# version 02
# baseline (speedup 1.0000x reference)
import numpy as np
import ml_dtypes
import concourse.bass as bass
import concourse.mybir as mybir
from concourse.bass_utils import run_bass_kernel_spmd

F32 = mybir.dt.float32
BF16 = mybir.dt.bfloat16
I32 = mybir.dt.int32
AF = mybir.ActivationFunctionType
ALU = mybir.AluOpType
AX = mybir.AxisListType.X
BF = ml_dtypes.bfloat16

B, S, D = 4, 4096, 1024
T = 2048
NH, HD, DROPE, LAT = 16, 64, 16, 256
NE, HMOE = 8, 1024
HFF = 3072
CAP = 2560.0
EPS = 1e-5

STAGE = 4
DEBUG = False
MOE_CUT = 9

ENGINES = ("sync", "vector", "scalar", "tensor", "gpsimd")


class Prog:
    def __init__(self, nc):
        self.nc = nc
        self.q = {e: [] for e in ENGINES}
        self.semh = {}
        self.cnt = {}

    def new_sem(self, ctx, name):
        self.semh[name] = ctx.enter_context(self.nc.semaphore(name))
        self.cnt[name] = 0

    def emit(self, eng, fn=None, inc=None, incby=1, wait=None):
        semh = self.semh
        ops = self.q[eng]
        if wait:
            wl = tuple((s, v) for s, v in wait if v > 0)
            if wl:
                def wfn(e, wl=wl):
                    for s, v in wl:
                        e.wait_ge(semh[s], v)
                ops.append(wfn)
        val = None
        if inc is not None:
            self.cnt[inc] += incby
            val = self.cnt[inc]
        if fn is not None:
            if inc is not None:
                ops.append(lambda e, fn=fn, s=inc, n=incby: fn(e).then_inc(semh[s], n))
            else:
                ops.append(lambda e, fn=fn: fn(e))
        return val

    def now(self, sem):
        return self.cnt[sem]

    def replay(self):
        with self.nc.Block() as block:
            for eng in ENGINES:
                ops = self.q[eng]
                def make(ops):
                    def run(e):
                        for f in ops:
                            f(e)
                    return run
                getattr(block, eng)(make(ops))


class View:
    __slots__ = ("t", "off", "parts", "cols")

    def __init__(self, t, off, parts, cols):
        self.t, self.off, self.parts, self.cols = t, off, parts, cols

    def __getitem__(self, key):
        ps, fs = key
        if isinstance(ps, slice):
            p0 = ps.start or 0
            p1 = self.parts if ps.stop is None else ps.stop
        else:
            p0, p1 = ps, ps + 1
        if isinstance(fs, slice):
            f0 = fs.start or 0
            f1 = self.cols if fs.stop is None else fs.stop
            st = fs.step or 1
        else:
            f0, f1, st = fs, fs + 1, 1
        if st != 1:
            return self.t[p0:p1, self.off + f0:self.off + f1:st]
        return self.t[p0:p1, self.off + f0:self.off + f1]


class Arena:
    def __init__(self, t, total):
        self.t, self.total, self.ptr, self.peak = t, total, 0, 0

    def alloc(self, cols, parts=128):
        cols = int(cols)
        off = self.ptr
        self.ptr += cols
        self.peak = max(self.peak, self.ptr)
        assert self.ptr <= self.total, f"arena overflow {self.ptr} > {self.total}"
        return View(self.t, off, parts, cols)

    def alloc_fm(self, n, cols, parts=128):
        return [self.alloc(cols, parts) for _ in range(n)]

    def mark(self):
        return self.ptr

    def release(self, m):
        self.ptr = m


def build_program(stage=STAGE, debug=DEBUG):
    from contextlib import ExitStack
    nc = bass.Bass("TRN2", target_bir_lowering=False,
                   detect_race_conditions=False)
    ctx = ExitStack()
    dram = {}

    def din(name, shape, dt):
        dram[name] = nc.dram_tensor(name, shape, dt, kind="ExternalInput").ap()
        return dram[name]

    def dout(name, shape, dt):
        dram[name] = nc.dram_tensor(name, shape, dt, kind="ExternalOutput").ap()
        return dram[name]

    def dint(name, shape, dt):
        dram[name] = nc.dram_tensor(name, shape, dt).ap()
        return dram[name]

    x_own = din("x_own", [T, D], F32)
    x_sib = din("x_sib", [T, D], F32)
    wqT = din("wqT", [D, D], F32)
    wkaT = din("wkaT", [D, LAT], F32)
    wkbT = din("wkbT", [LAT, D], F32)
    wvaT = din("wvaT", [D, LAT], F32)
    wvbT = din("wvbT", [LAT, D], F32)
    woT = din("woT", [D, D], F32)
    router_w = din("router_w", [D, NE], F32)
    router_b = din("router_b", [1, NE], F32)
    w1d = din("w1", [NE, D, HMOE], F32)
    b1d = din("b1", [NE, HMOE], F32)
    w2d = din("w2", [NE, HMOE, D], F32)
    b2d = din("b2", [NE, D], F32)
    l1wqT = din("l1wqT", [D, D], F32)
    l1wkT = din("l1wkT", [D, D], F32)
    l1wvT = din("l1wvT", [D, D], F32)
    l1woT = din("l1woT", [D, D], F32)
    wgT = din("wgT", [D, HFF], I32)
    wuT = din("wuT", [D, HFF], I32)
    wdT = din("wdT", [HFF, D], I32)
    identc = din("identc", [128, 128], BF16)
    onesc = din("onesc", [128, 1], BF16)
    onesrow = din("onesrow", [1, 128], F32)
    ropeCc = din("ropeC", [128, 512], BF16)
    ropeSc = din("ropeS", [128, 512], BF16)
    iscore0 = din("iscore0", [128, 1], F32)
    onesrowbc = din("onesrowb", [1, 128], BF16)
    b1Td = din("b1T", [HMOE, NE], F32)
    cparams = din("cparams", [1, 2], I32)

    y = dout("y", [T, D], F32)
    if debug:
        dout("dbg_x1", [T, D], F32)
        dout("dbg_x2", [T, D], F32)
        dout("dbg_x3", [T, D], F32)
        dout("dbg_q", [8 * 128, T], BF16)
        dout("dbg_klat", [2 * 128, S], BF16)
        dout("dbg_vlat", [2 * 128, S], BF16)
        dout("dbg_ot", [8 * 128, T], BF16)
        dout("dbg_xn", [8 * 128, D], BF16)
        dout("dbg_stats", [2 * 128, 16], F32)
        dout("dbg_xst", [2 * 128, D], F32)

    dint("ag_in", [T, D], F32)
    dint("ag_out", [S, D], F32)
    dint("ar_in", [16, 1], F32)
    dint("ar_out", [16, 1], F32)

    AF32 = 18560
    ABF = 64448
    a32t = ctx.enter_context(nc.sbuf_tensor("a32", [128, AF32], F32))
    abft = ctx.enter_context(nc.sbuf_tensor("abf", [128, ABF], BF16))
    a32 = Arena(a32t, AF32)
    abf = Arena(abft, ABF)

    s_ident = ctx.enter_context(nc.sbuf_tensor("s_ident", [128, 128], BF16))
    s_zero = ctx.enter_context(nc.sbuf_tensor("s_zero", [128, 128], BF16))
    s_ones = ctx.enter_context(nc.sbuf_tensor("s_ones", [128, 1], BF16))
    s_onesrow = ctx.enter_context(nc.sbuf_tensor("s_onesrow", [1, 128], F32))
    onesrowb = ctx.enter_context(nc.sbuf_tensor("s_onesrowb", [1, 128], BF16))
    s_ropeC = ctx.enter_context(nc.sbuf_tensor("s_ropeC", [128, 512], BF16))
    s_ropeS = ctx.enter_context(nc.sbuf_tensor("s_ropeS", [128, 512], BF16))
    s_isc0 = ctx.enter_context(nc.sbuf_tensor("s_isc0", [128, 1], F32))
    s_cpar = ctx.enter_context(nc.sbuf_tensor("s_cpar", [1, 2], I32))
    s_i32 = ctx.enter_context(nc.sbuf_tensor("s_i32", [128, 1536], I32))

    pf = [ctx.enter_context(nc.psum_tensor(f"pf{i}", [128, 512], F32)) for i in range(6)]
    pbf = ctx.enter_context(nc.psum_tensor("pbf", [128, 1024], BF16))

    p = Prog(nc)
    for snm in ("CN", "XA", "XB", "WA", "WB", "WL", "TE", "VE", "SE",
                "CC", "ST", "SG", "GL", "GA", "GB", "IA", "IB"):
        p.new_sem(ctx, snm)
    E = p.emit

    def dma(eng, sem, out, in_, wait=None):
        return E(eng, lambda e, o=out, i=in_: e.dma_start(out=o, in_=i),
                 inc=sem, incby=16, wait=wait)

    def mm(out, lhsT, rhs, start, stop, inc=None, wait=None, tp=None):
        return E("tensor",
                 lambda e, o=out, l=lhsT, r=rhs, st=start, sp=stop, tp=tp:
                 e.matmul(o, l, r, start=st, stop=sp, tile_position=tp),
                 inc=inc, wait=wait)

    def tr(out, in_, inc="TE", wait=None, idn=None):
        idap = s_ident[:, :] if idn is None else idn
        return E("tensor", lambda e, o=out, i=in_, idap=idap: e.transpose(o, i, idap),
                 inc=inc, wait=wait)

    def vop(fn, inc="VE", wait=None):
        return E("vector", fn, inc=inc, wait=wait)

    def vsync():
        E("vector", wait=[("VE", p.now("VE"))])

    def sop(fn, inc="SE", wait=None):
        return E("scalar", fn, inc=inc, wait=wait)

    xstream = a32.alloc_fm(16, D)
    xstage = [a32.alloc(D), a32.alloc(D)]
    stats = a32.alloc(32)
    csc = a32.alloc(32)
    cb = a32.alloc(1)
    sumsacc = a32.alloc(16)

    dma("sync", "CN", s_ident[:, :], identc)
    E("vector", lambda e: e.memset(s_zero[:, :], 0.0), inc="VE")
    dma("sync", "CN", s_ones[:, :], onesc)
    dma("sync", "CN", s_onesrow[:, :], onesrow)
    dma("sync", "CN", onesrowb[:, :], onesrowbc)
    dma("sync", "CN", s_ropeC[:, :], ropeCc)
    dma("sync", "CN", s_ropeS[:, :], ropeSc)
    dma("sync", "CN", s_isc0[:, :], iscore0)
    dma("sync", "CN", s_cpar[:, :], cparams)
    consts_xl = p.now("CN")
    CONSTS_CN = consts_xl

    SW = {}

    def alloc_sweepbufs():
        SW["xnT"] = abf.alloc_fm(8, 512)
        SW["xnbf"] = [abf.alloc(D), abf.alloc(D)]
        SW["dump"] = [abf.alloc(D), abf.alloc(D)]

    mk_L0 = abf.mark()
    Qfm = abf.alloc_fm(8, T)
    wkb_r = abf.alloc_fm(2, D)
    wvb_b = abf.alloc_fm(2, D)
    klat = abf.alloc_fm(2, S)
    vlat = abf.alloc_fm(2, S)
    mk_sweep0 = abf.mark()
    alloc_sweepbufs()
    mk_W = abf.mark()
    wq_r = abf.alloc_fm(8, D)
    wka_b = abf.alloc_fm(8, LAT)
    wva_b = abf.alloc_fm(8, LAT)
    mk_raw = abf.mark()
    wq_raw = [abf.alloc(D), abf.alloc(D)]
    tmp_r = [abf.alloc(512), abf.alloc(512)]

    for i in range(8):
        dma("gpsimd", "WL", wka_b[i][:, :], wkaT[128 * i:128 * (i + 1), :])
        dma("gpsimd", "WL", wva_b[i][:, :], wvaT[128 * i:128 * (i + 1), :])
    for i in range(2):
        dma("gpsimd", "WL", wvb_b[i][:, :], wvbT[128 * i:128 * (i + 1), :])
    l0w_wl = p.now("WL")

    def rope_fold(raw, out, first_wait):
        We, Wo = raw[:, 0:1024:2], raw[:, 1:1024:2]
        vop(lambda e: e.tensor_tensor(tmp_r[0][:, :], We, s_ropeC[:, :], ALU.mult),
            wait=first_wait)
        vop(lambda e: e.tensor_tensor(tmp_r[1][:, :], Wo, s_ropeS[:, :], ALU.mult))
        vop(lambda e: e.tensor_tensor(out[:, 0:1024:2], tmp_r[0][:, :], tmp_r[1][:, :], ALU.subtract))
        vop(lambda e: e.tensor_tensor(tmp_r[0][:, :], We, s_ropeS[:, :], ALU.mult))
        vop(lambda e: e.tensor_tensor(tmp_r[1][:, :], Wo, s_ropeC[:, :], ALU.mult))
        vop(lambda e: e.tensor_tensor(out[:, 1:1024:2], tmp_r[0][:, :], tmp_r[1][:, :], ALU.add))

    for i in range(10):
        src_hbm = wqT[128 * i:128 * (i + 1), :] if i < 8 else \
                  wkbT[128 * (i - 8):128 * (i - 7), :]
        dst = wq_r[i] if i < 8 else wkb_r[i - 8]
        psem = "WA" if i % 2 == 0 else "WB"
        dma("gpsimd", psem, wq_raw[i % 2][:, :], src_hbm,
            wait=[("VE", p.now("VE"))] if i >= 2 else None)
        wl = p.now(psem)
        fw = [(psem, wl)] if i >= 1 else [(psem, wl), ("CN", consts_xl)]
        rope_fold(wq_raw[i % 2], dst, fw)

    def ln_stats(src, par, wait_s):
        sl = par * 16
        w = list(wait_s or []) + [("VE", p.now("VE")), ("TE", p.now("TE"))]
        xb = SW["xnbf"][par]
        du = SW["dump"][par]
        sop(lambda e, xb=xb: e.activation(xb[:, :], src[:, :], AF.Copy,
                                          accum_out=stats[:, sl + 0:sl + 1]), wait=w)
        sop(lambda e, du=du: e.activation(du[:, :], src[:, :], AF.Square,
                                          accum_out=stats[:, sl + 1:sl + 2]))
        return p.now("SE")

    def ln_norm(src, dst_bf, par, se):
        sl = par * 16
        vop(lambda e: e.tensor_scalar(stats[:, sl + 2:sl + 3], stats[:, sl + 0:sl + 1],
                                      1.0 / D, None, ALU.mult), wait=[("SE", se)])
        vsync()
        vop(lambda e: e.scalar_tensor_tensor(stats[:, sl + 3:sl + 4], stats[:, sl + 2:sl + 3],
                                             -1.0, stats[:, sl + 2:sl + 3],
                                             ALU.mult, ALU.mult))
        vsync()
        vop(lambda e: e.tensor_scalar(stats[:, sl + 3:sl + 4], stats[:, sl + 3:sl + 4],
                                      EPS, None, ALU.add))
        vsync()
        vop(lambda e: e.scalar_tensor_tensor(stats[:, sl + 4:sl + 5], stats[:, sl + 1:sl + 2],
                                             1.0 / D, stats[:, sl + 3:sl + 4],
                                             ALU.mult, ALU.add))
        ve = p.now("VE")
        sop(lambda e: e.activation(stats[:, sl + 5:sl + 6], stats[:, sl + 4:sl + 5],
                                   AF.Sqrt), wait=[("VE", ve)])
        se2 = p.now("SE")
        vop(lambda e: e.reciprocal(stats[:, sl + 6:sl + 7], stats[:, sl + 5:sl + 6]),
            wait=[("SE", se2)])
        vsync()
        vop(lambda e: e.tensor_scalar(dst_bf[:, :], src[:, :], stats[:, sl + 2:sl + 3],
                                      stats[:, sl + 6:sl + 7], ALU.subtract, ALU.mult))
        return p.now("VE")

    SERIAL_SWEEP = False

    def sweep(n_chunks, get_src, on_group, group_sz=4, dst_of=None):
        pend = None
        for tc in range(n_chunks + 1):
            if tc < n_chunks:
                if SERIAL_SWEEP:
                    E("scalar", wait=[("VE", p.now("VE")), ("TE", p.now("TE")),
                                      ("SG", p.now("SG"))])
                src, xl_w = get_src(tc)
                se = ln_stats(src, tc % 2, xl_w)
                cur = (tc, src, se)
            else:
                cur = None
            if SERIAL_SWEEP and pend is None:
                pend = cur
                continue
            if pend is not None:
                t0, src0, se0 = pend
                par = t0 % 2
                ve = ln_norm(src0, SW["xnbf"][par], par, se0)
                if debug and SERIAL_SWEEP and t0 < 8 and n_chunks == 32:
                    dma("gpsimd", "SG", dram["dbg_xn"][128 * t0:128 * (t0 + 1), :],
                        SW["xnbf"][par][:, :], wait=[("VE", ve)])
                    if t0 < 2:
                        dma("gpsimd", "SG", dram["dbg_stats"][128 * t0:128 * (t0 + 1), :],
                            stats[:, 16 * par:16 * par + 16])
                        dma("gpsimd", "SG", dram["dbg_xst"][128 * t0:128 * (t0 + 1), :],
                            src0[:, :])
                for c8 in range(8):
                    tr(pbf[:, 128 * c8:128 * (c8 + 1)],
                       SW["xnbf"][par][:, 128 * c8:128 * (c8 + 1)],
                       wait=[("VE", ve)] if c8 == 0 else None)
                te = p.now("TE")
                g4 = t0 % group_sz
                if dst_of is not None:
                    dsts = dst_of(t0)
                else:
                    dsts = [View(v.t, v.off + 128 * g4, 128, 128)
                            for v in SW["xnT"]]
                for c8 in range(8):
                    vop(lambda e, c8=c8, d=dsts[c8]: e.tensor_copy(
                        d[:, 0:128], pbf[:, 128 * c8:128 * (c8 + 1)]),
                        wait=[("TE", te)] if c8 == 0 else None)
                evac_ve = p.now("VE")
                E("tensor", wait=[("VE", evac_ve)])
                if (t0 % group_sz == group_sz - 1) and on_group is not None:
                    on_group(t0 // group_sz)
            pend = cur
            if SERIAL_SWEEP and cur is not None:
                t0, src0, se0 = cur
                par = t0 % 2
                ve = ln_norm(src0, SW["xnbf"][par], par, se0)
                if debug and t0 < 8 and n_chunks == 32:
                    dma("gpsimd", "SG", dram["dbg_xn"][128 * t0:128 * (t0 + 1), :],
                        SW["xnbf"][par][:, :], wait=[("VE", ve)])
                    if t0 < 2:
                        dma("gpsimd", "SG", dram["dbg_stats"][128 * t0:128 * (t0 + 1), :],
                            stats[:, 16 * par:16 * par + 16])
                        dma("gpsimd", "SG", dram["dbg_xst"][128 * t0:128 * (t0 + 1), :],
                            src0[:, :])
                for c8 in range(8):
                    tr(pbf[:, 128 * c8:128 * (c8 + 1)],
                       SW["xnbf"][par][:, 128 * c8:128 * (c8 + 1)],
                       wait=[("VE", ve)] if c8 == 0 else None)
                te = p.now("TE")
                g4 = t0 % group_sz
                if dst_of is not None:
                    dsts = dst_of(t0)
                else:
                    dsts = [View(v.t, v.off + 128 * g4, 128, 128)
                            for v in SW["xnT"]]
                for c8 in range(8):
                    vop(lambda e, c8=c8, d=dsts[c8]: e.tensor_copy(
                        d[:, 0:128], pbf[:, 128 * c8:128 * (c8 + 1)]),
                        wait=[("TE", te)] if c8 == 0 else None)
                E("tensor", wait=[("VE", p.now("VE"))])
                if (t0 % group_sz == group_sz - 1) and on_group is not None:
                    on_group(t0 // group_sz)
                pend = None

    E("tensor", wait=[("CN", CONSTS_CN)])
    vop(lambda e: e.memset(csc[:, :], 0.0))
    E("scalar", wait=[("VE", p.now("VE"))])
    for fn_ in (AF.Copy, AF.Square, AF.Sqrt, AF.Exp, AF.Gelu, AF.Silu):
        sop(lambda e, fn_=fn_: e.activation(csc[:, 21:22], csc[:, 22:23], fn_))
    warm_se = p.now("SE")
    E("scalar", wait=[("SE", warm_se)])

    def x0_src(tc):
        own = tc < 16
        hb = (x_own if own else x_sib)[128 * (tc % 16):128 * (tc % 16 + 1), :]
        st = xstage[tc % 2]
        psem = "XA" if tc % 2 == 0 else "XB"
        dma("sync", psem, st[:, :], hb,
            wait=[("VE", p.now("VE"))] if tc >= 2 else None)
        return st, [(psem, p.now(psem))]

    def x0_group(g):
        grp = SW["xnT"]
        base_w = [("WL", l0w_wl), ("VE", p.now("VE"))]
        first = True
        if g < 4:
            for fch in range(8):
                bank = pf[fch % 2]
                for dch in range(8):
                    w = None
                    if first and dch == 0:
                        w = base_w
                        first = False
                    elif dch == 0:
                        w = [("VE", p.now("VE"))]
                    mm(bank[:, :], wq_r[dch][:, 128 * fch:128 * (fch + 1)],
                       grp[dch][:, 0:512], start=(dch == 0), stop=(dch == 7),
                       inc="TE" if dch == 7 else None, wait=w)
                te = p.now("TE")
                vop(lambda e, fch=fch, g=g, bank=bank: e.tensor_copy(
                    Qfm[fch][:, 512 * g:512 * (g + 1)], bank[:, :]), wait=[("TE", te)])
        for wv, dst, pb in ((wka_b, klat, 2), (wva_b, vlat, 3)):
            for lch in range(2):
                bank = pf[pb]
                for dch in range(8):
                    w = None
                    if first and dch == 0:
                        w = base_w
                        first = False
                    elif dch == 0:
                        w = [("VE", p.now("VE"))]
                    mm(bank[:, :], wv[dch][:, 128 * lch:128 * (lch + 1)],
                       grp[dch][:, 0:512], start=(dch == 0), stop=(dch == 7),
                       inc="TE" if dch == 7 else None, wait=w)
                te = p.now("TE")
                vop(lambda e, dst=dst, lch=lch, g=g, bank=bank: e.tensor_copy(
                    dst[lch][:, 512 * g:512 * (g + 1)], bank[:, :]), wait=[("TE", te)])

    abf.release(mk_raw)
    sweep(32, x0_src, x0_group)
    abf.release(mk_sweep0)
    if debug:
        dve = p.now("VE")
        for i in range(8):
            dma("gpsimd", "SG", dram["dbg_q"][128 * i:128 * (i + 1), :],
                Qfm[i][:, :], wait=[("VE", dve)] if i == 0 else None)
        for i in range(2):
            dma("gpsimd", "SG", dram["dbg_klat"][128 * i:128 * (i + 1), :],
                klat[i][:, :])
            dma("gpsimd", "SG", dram["dbg_vlat"][128 * i:128 * (i + 1), :],
                vlat[i][:, :])

    OT = abf.alloc_fm(8, T)
    mk_att = abf.mark()
    Kh = abf.alloc(S)
    Vh = abf.alloc(32 * 65)
    PT = [abf.alloc(1024), abf.alloc(1024)]
    scr = abf.alloc(512, parts=64)
    otile = [abf.alloc(64), abf.alloc(64)]

    vop(lambda e: e.memset(Vh[:, :], 1.0))
    proj_ve = p.now("VE")
    E("tensor", wait=[("VE", proj_ve)])

    for h in range(NH):
        cch, base = h // 2, (h % 2) * 64
        tp = None if base == 0 else (64, 0)
        E("tensor", wait=[("SE", p.now("SE"))])
        for cc in range(8):
            bank = pf[cc % 2]
            for lch in range(2):
                mm(bank[0:64, :], wkb_r[lch][:, 64 * h:64 * (h + 1)],
                   klat[lch][:, 512 * cc:512 * (cc + 1)],
                   start=(lch == 0), stop=(lch == 1),
                   inc="TE" if lch == 1 else None,
                   wait=[("VE", p.now("VE"))] if (lch == 0 and cc >= 2) else None)
            te = p.now("TE")
            vop(lambda e, cc=cc, bank=bank, base=base: e.tensor_copy(
                Kh[base:base + 64, 512 * cc:512 * (cc + 1)], bank[0:64, :]),
                wait=[("TE", te)])
        for kc in range(32):
            bank = pf[2 + kc % 2]
            for lch in range(2):
                mm(bank[:, 0:64], vlat[lch][:, 128 * kc:128 * (kc + 1)],
                   wvb_b[lch][:, 64 * h:64 * (h + 1)],
                   start=(lch == 0), stop=(lch == 1),
                   inc="TE" if lch == 1 else None,
                   wait=[("VE", p.now("VE"))] if (lch == 0 and kc >= 2) else None)
            te = p.now("TE")
            vop(lambda e, kc=kc, bank=bank: e.tensor_copy(
                Vh[:, 65 * kc:65 * kc + 64], bank[:, 0:64]), wait=[("TE", te)])
        for i in range(12):
            if i < 4:
                sq = Qfm[cch][base:base + 64, 512 * i:512 * (i + 1)]
            else:
                sq = Kh[base:base + 64, 512 * (i - 4):512 * (i - 3)]
            vop(lambda e, sq=sq: e.tensor_tensor(scr[0:64, :], sq, sq, ALU.mult))
            ve = p.now("VE")
            mm(pf[0][0:1, :], s_ones[0:64, 0:1], scr[0:64, :], start=True, stop=True,
               inc="TE", wait=[("VE", ve)])
            te = p.now("TE")
            vop(lambda e, i=i: e.reduce_max(csc[0:1, i:i + 1], pf[0][0:1, :], AX),
                wait=[("TE", te)])
            E("tensor", wait=[("VE", p.now("VE"))])
        vsync()
        vop(lambda e: e.reduce_max(csc[0:1, 13:14], csc[0:1, 0:4], AX))
        vop(lambda e: e.reduce_max(csc[0:1, 14:15], csc[0:1, 4:12], AX))
        vsync()
        vop(lambda e: e.tensor_tensor(csc[0:1, 15:16], csc[0:1, 13:14],
                                      csc[0:1, 14:15], ALU.add))
        vsync()
        vop(lambda e: e.tensor_scalar(csc[0:1, 16:17], csc[0:1, 15:16],
                                      -1.0 / 16.0, None, ALU.mult))
        ve = p.now("VE")
        mm(pf[0][:, 0:1], s_onesrow[0:1, :], csc[0:1, 16:17], start=True, stop=True,
           inc="TE", wait=[("VE", ve)])
        te = p.now("TE")
        vop(lambda e: e.tensor_copy(cb[:, 0:1], pf[0][:, 0:1]),
            wait=[("TE", te), ("SE", p.now("SE"))])
        cb_ve = p.now("VE")
        E("tensor", wait=[("VE", cb_ve)])

        for q4 in range(4):
            q0 = 512 * q4
            se_base = p.now("SE")
            exp_marks = [se_base, se_base]
            av_done_ve = p.now("VE")
            SERIAL_ATT = False
            for kt in range(33):
                if kt < 32:
                    par = kt % 2
                    sb = pf[par]
                    wt = [("SE", exp_marks[par])]
                    if kt == 0:
                        wt.append(("VE", av_done_ve))
                    if SERIAL_ATT:
                        wt = [("SE", p.now("SE")), ("VE", p.now("VE"))]
                    mm(sb[:, :], Kh[base:base + 64, 128 * kt:128 * (kt + 1)],
                       Qfm[cch][base:base + 64, q0:q0 + 512],
                       start=True, stop=True, wait=wt, tp=tp, inc="TE")
                    te = p.now("TE")
                    sop(lambda e, par=par, sb=sb: e.activation(
                        PT[par][:, 0:512], sb[:, :], AF.Exp, bias=cb[:, 0:1],
                        scale=0.125), wait=[("TE", te)])
                    exp_marks[par] = p.now("SE")
                if kt >= 1:
                    k0 = kt - 1
                    par0 = k0 % 2
                    for qc in range(4):
                        mm(pf[2 + qc][:, 0:65],
                           PT[par0][:, 128 * qc:128 * (qc + 1)],
                           Vh[:, 65 * k0:65 * k0 + 65],
                           start=(k0 == 0), stop=(k0 == 31),
                           inc="TE" if qc == 3 else None,
                           wait=[("SE", exp_marks[par0])] if qc == 0 else None)
            te_av = p.now("TE")
            for qc in range(4):
                dstb = pf[2 + qc]
                tch = 4 * q4 + qc
                opar = qc % 2
                vop(lambda e, dstb=dstb: e.reciprocal(
                    csc[:, 20:21], dstb[:, 64:65]),
                    wait=[("TE", te_av)] if qc == 0 else None)
                vsync()
                w2_ = [("TE", p.now("TE"))] if qc >= 2 else None
                vop(lambda e, dstb=dstb, opar=opar: e.tensor_scalar(
                    otile[opar][:, 0:64], dstb[:, 0:64],
                    csc[:, 20:21], None, ALU.mult), wait=w2_)
                ve = p.now("VE")
                tr(pbf[0:64, 128 * opar:128 * opar + 128],
                   otile[opar][:, 0:64], wait=[("VE", ve)])
                te2 = p.now("TE")
                vop(lambda e, base=base, cch=cch, tch=tch, opar=opar: e.tensor_copy(
                    OT[cch][base:base + 64, 128 * tch:128 * (tch + 1)],
                    pbf[0:64, 128 * opar:128 * opar + 128]), wait=[("TE", te2)])
            E("tensor", wait=[("VE", p.now("VE"))])

    if debug:
        dve = p.now("VE")
        for i in range(8):
            dma("gpsimd", "SG", dram["dbg_ot"][128 * i:128 * (i + 1), :],
                OT[i][:, :], wait=[("VE", dve)] if i == 0 else None)
    abf.release(mk_att)

    mk_wo = abf.mark()
    wo_b = abf.alloc_fm(8, D)
    E("gpsimd", wait=[("WL", p.now("WL")), ("VE", p.now("VE")),
                      ("TE", p.now("TE")), ("SE", p.now("SE"))])
    for i in range(8):
        dma("gpsimd", "WL", wo_b[i][:, :], woT[128 * i:128 * (i + 1), :])
    wo_wl = p.now("WL")

    for tch in range(16):
        psem = "XA" if tch % 2 == 0 else "XB"
        dma("sync", psem, xstage[tch % 2][:, :], x_own[128 * tch:128 * (tch + 1), :],
            wait=[("VE", p.now("VE"))])
        xl = (psem, p.now(psem))
        for oh in range(2):
            bank = pf[oh]
            for dch in range(8):
                w = None
                if dch == 0:
                    w = [("WL", wo_wl), ("VE", p.now("VE"))] if tch < 2 else \
                        [("VE", p.now("VE"))]
                mm(bank[:, :], OT[dch][:, 128 * tch:128 * (tch + 1)],
                   wo_b[dch][:, 512 * oh:512 * (oh + 1)],
                   start=(dch == 0), stop=(dch == 7),
                   inc="TE" if dch == 7 else None, wait=w)
            te = p.now("TE")
            vop(lambda e, tch=tch, oh=oh, bank=bank: e.tensor_tensor(
                xstream[tch][:, 512 * oh:512 * (oh + 1)], bank[:, :],
                xstage[tch % 2][:, 512 * oh:512 * (oh + 1)], ALU.add),
                wait=[("TE", te), xl])
    x1_ve = p.now("VE")
    abf.release(mk_L0)

    if debug:
        for tch in range(16):
            dma("gpsimd", "SG", dram["dbg_x1"][128 * tch:128 * (tch + 1), :],
                xstream[tch][:, :], wait=[("VE", x1_ve)] if tch == 0 else None)
        E("vector", wait=[("SG", p.now("SG"))])

    def moe_phase():
        mkM = abf.mark()
        alloc_sweepbufs()
        xn1T = abf.alloc_fm(8, T)
        rwb = abf.alloc_fm(8, NE)
        maskTb = abf.alloc(T, parts=8)
        mcbT = abf.alloc(8, parts=8)
        mcb8 = abf.alloc(8, parts=8)
        b2sb = abf.alloc(D, parts=8)
        mb = abf.alloc(512)
        w1b = abf.alloc_fm(8, HMOE)
        w2b = abf.alloc_fm(8, D)
        he = abf.alloc_fm(8, 512)
        t1b = abf.alloc(512)
        gtb = abf.alloc(512)
        mrow = abf.alloc(T, parts=1)
        ovl = xstage[0]
        logit = View(ovl.t, ovl.off + 0, 128, 8)
        uexp = View(ovl.t, ovl.off + 8, 128, 8)
        eq1 = View(ovl.t, ovl.off + 16, 128, 8)
        u2 = View(ovl.t, ovl.off + 24, 128, 8)
        ge2 = View(ovl.t, ovl.off + 32, 128, 8)
        maskf = View(ovl.t, ovl.off + 40, 128, 8)
        m01 = View(ovl.t, ovl.off + 48, 128, 16)
        sm8 = View(ovl.t, ovl.off + 64, 128, 8)
        rbb = View(ovl.t, ovl.off + 72, 128, 8)
        onesf = View(ovl.t, ovl.off + 80, 128, 1)
        s01g = View(ovl.t, ovl.off + 81, 16, 1)
        s01g2 = View(ovl.t, ovl.off + 82, 16, 1)
        s01ls = View(ovl.t, ovl.off + 83, 16, 1)
        mrows = View(ovl.t, ovl.off + 84, 8, 8)
        mcf = View(ovl.t, ovl.off + 92, 8, 8)
        rbs = View(ovl.t, ovl.off + 100, 1, NE)
        dscr = View(ovl.t, ovl.off + 108, 8, 512)
        b1Ts = [View(ovl.t, ovl.off + 620 + 8 * i, 128, 8) for i in range(8)]

        E("gpsimd", wait=[("WL", p.now("WL")), ("VE", p.now("VE")),
                          ("TE", p.now("TE")), ("SE", p.now("SE"))])
        for i in range(8):
            dma("gpsimd", "WL", rwb[i][:, :], router_w[128 * i:128 * (i + 1), :])
        dma("gpsimd", "WL", b2sb[0:8, :], b2d[:, :])
        for i in range(8):
            dma("gpsimd", "WL", b1Ts[i][:, :], b1Td[128 * i:128 * (i + 1), :])
        dma("sync", "CN", rbs[0:1, :], router_b,
            wait=[("VE", p.now("VE")), ("SE", p.now("SE"))])
        cn_rb = p.now("CN")
        moe_wl = p.now("WL")
        vop(lambda e: e.memset(onesf[:, 0:1], 1.0))

        def m_src(tc):
            return xstream[tc], None

        def m_dst(t0):
            return [View(v.t, v.off + 128 * t0, 128, 128) for v in xn1T]

        import kernel as _K
        if _K.MOE_CUT < 2:
            abf.release(mkM)
            return
        sweep(16, m_src, None, dst_of=m_dst)
        swe_ve = p.now("VE")
        if _K.MOE_CUT < 3:
            abf.release(mkM)
            return

        mm(pf[4][:, 0:8], s_onesrow[0:1, :], rbs[0:1, :], start=True, stop=True,
           inc="TE", wait=[("CN", cn_rb), ("VE", swe_ve)])
        te = p.now("TE")
        vop(lambda e: e.tensor_copy(rbb[:, 0:8], pf[4][:, 0:8]), wait=[("TE", te)])

        for i in range(16):
            bank = pf[i % 2]
            for dch in range(8):
                mm(bank[:, 0:8], xn1T[dch][:, 128 * i:128 * (i + 1)],
                   rwb[dch][:, :], start=(dch == 0), stop=(dch == 7),
                   inc="TE" if dch == 7 else None,
                   wait=[("WL", moe_wl), ("VE", p.now("VE"))] if dch == 0 else None)
            te = p.now("TE")
            vop(lambda e, bank=bank: e.tensor_tensor(logit[:, :], bank[:, 0:8],
                                                     rbb[:, 0:8], ALU.add),
                wait=[("TE", te)])
            vsync()
            vop(lambda e: e.reduce_max(sm8[:, 0:1], logit[:, :], AX))
            vsync()
            vop(lambda e: e.tensor_scalar(sm8[:, 1:2], sm8[:, 0:1], -1.0, None,
                                          ALU.mult))
            ve = p.now("VE")
            sop(lambda e: e.activation(uexp[:, :], logit[:, :], AF.Exp,
                                       bias=sm8[:, 1:2]), wait=[("VE", ve)])
            se = p.now("SE")
            vop(lambda e: e.reduce_max(sm8[:, 2:3], uexp[:, :], AX),
                wait=[("SE", se)])
            vsync()
            vop(lambda e: e.tensor_scalar(eq1[:, :], uexp[:, :], sm8[:, 2:3],
                                          None, ALU.is_equal))
            vsync()
            vop(lambda e: e.scalar_tensor_tensor(u2[:, :], eq1[:, :], -1e30,
                                                 uexp[:, :], ALU.mult, ALU.add))
            vsync()
            vop(lambda e: e.reduce_max(sm8[:, 3:4], u2[:, :], AX))
            vsync()
            vop(lambda e: e.tensor_tensor(sm8[:, 4:5], sm8[:, 2:3], sm8[:, 3:4],
                                          ALU.add))
            vsync()
            vop(lambda e: e.reciprocal(sm8[:, 5:6], sm8[:, 4:5]))
            vsync()
            vop(lambda e: e.tensor_scalar(ge2[:, :], uexp[:, :], sm8[:, 3:4],
                                          None, ALU.is_ge))
            vop(lambda e: e.tensor_scalar(maskf[:, :], uexp[:, :], sm8[:, 5:6],
                                          None, ALU.mult))
            vsync()
            vop(lambda e: e.tensor_tensor(maskf[:, :], maskf[:, :], ge2[:, :],
                                          ALU.mult))
            vop(lambda e: e.tensor_tensor(sm8[:, 6:7], sm8[:, 2:3], sm8[:, 5:6],
                                          ALU.mult))
            vop(lambda e: e.tensor_tensor(sm8[:, 7:8], sm8[:, 3:4], sm8[:, 5:6],
                                          ALU.mult))
            vsync()
            vop(lambda e: e.tensor_scalar(m01[:, 0:8], eq1[:, :], sm8[:, 6:7],
                                          None, ALU.mult))
            vop(lambda e: e.tensor_tensor(u2[:, :], ge2[:, :], eq1[:, :],
                                          ALU.subtract))
            vsync()
            vop(lambda e: e.tensor_scalar(m01[:, 8:16], u2[:, :], sm8[:, 7:8],
                                          None, ALU.mult))
            ve = p.now("VE")
            mm(pf[2][0:16, 0:1], m01[:, :], onesf[:, 0:1],
               start=(i == 0), stop=(i == 15), inc="TE", wait=[("VE", ve)])
            if i == 0:
                vop(lambda e: e.tensor_copy(mrows[0:8, :], maskf[0:8, :]))
            du0 = SW["dump"][0]
            vop(lambda e, du0=du0: e.tensor_copy(du0[:, 0:8], maskf[:, :]))
            ve = p.now("VE")
            tr(pbf[0:8, 0:128], du0[:, 0:8], wait=[("VE", ve)])
            te = p.now("TE")
            vop(lambda e, i=i: e.tensor_copy(maskTb[0:8, 128 * i:128 * (i + 1)],
                                             pbf[0:8, 0:128]), wait=[("TE", te)])
            E("tensor", wait=[("VE", p.now("VE"))])
        te = p.now("TE")
        vop(lambda e: e.tensor_copy(s01ls[0:16, 0:1], pf[2][0:16, 0:1]),
            wait=[("TE", te)])
        s01_ve = p.now("VE")
        dma("gpsimd", "GL", dram["ar_in"][:, :], s01ls[0:16, 0:1],
            wait=[("VE", s01_ve)])
        gl = p.now("GL")
        E("gpsimd", lambda e: e.collective_compute(
            "AllReduce", ALU.add, replica_groups=[list(range(8))],
            ins=[dram["ar_in"]], outs=[dram["ar_out"]]),
          inc="CC", wait=[("GL", gl)])
        ar_cc = p.now("CC")
        if _K.MOE_CUT < 4:
            abf.release(mkM)
            return

        def load_expert(e_i, extra_wait=None):
            wg = [("WL", p.now("WL")), ("TE", p.now("TE"))]
            if extra_wait:
                wg += extra_wait
            for i in range(8):
                dma("gpsimd", "WL", w1b[i][:, :],
                    w1d[e_i, 128 * i:128 * (i + 1), :],
                    wait=wg if i == 0 else None)
            for i in range(8):
                dma("gpsimd", "WL", w2b[i][:, :],
                    w2d[e_i, 128 * i:128 * (i + 1), :])
            return p.now("WL")

        def stage_mrow(e_i, mfm, ncols):
            dma("sync", "IA", mrow[0:1, 0:ncols], mfm[e_i:e_i + 1, 0:ncols],
                wait=[("TE", p.now("TE")), ("VE", p.now("VE"))])
            return p.now("IA")

        def build_he(e_i, t0, tw, ew, mfm, mr_ia):
            mm(pf[0][:, 0:tw], onesrowb[0:1, :], mrow[0:1, t0:t0 + tw],
               start=True, stop=True, inc="TE",
               wait=[("VE", p.now("VE")), ("SE", p.now("SE")), ("IA", mr_ia)])
            te = p.now("TE")
            vop(lambda e: e.tensor_copy(mb[:, 0:tw], pf[0][:, 0:tw]),
                wait=[("TE", te)])
            mb_ve = p.now("VE")
            for hc in range(8):
                bank = pf[hc % 2]
                for dch in range(8):
                    w = None
                    if dch == 0:
                        w = [("WL", ew), ("SE", p.now("SE")), ("VE", p.now("VE"))]
                    mm(bank[:, 0:tw], w1b[dch][:, 128 * hc:128 * (hc + 1)],
                       xn1T[dch][:, t0:t0 + tw],
                       start=(dch == 0), stop=(dch == 7),
                       inc="TE" if dch == 7 else None, wait=w)
                te = p.now("TE")
                vop(lambda e, bank=bank, tw=tw: e.tensor_tensor(
                    t1b[:, 0:tw], bank[:, 0:tw], mb[:, 0:tw], ALU.mult),
                    wait=[("TE", te)])
                ve = p.now("VE")
                sop(lambda e, hc=hc, tw=tw: e.activation(
                    gtb[:, 0:tw], t1b[:, 0:tw], AF.Gelu,
                    bias=b1Ts[hc][:, e_i:e_i + 1]), wait=[("VE", ve)])
                se = p.now("SE")
                vop(lambda e, hc=hc, tw=tw: e.tensor_tensor(
                    he[hc][:, 0:tw], gtb[:, 0:tw], mb[:, 0:tw], ALU.mult),
                    wait=[("SE", se)])
            return p.now("VE")

        for e_i in range(NE):
            ew = load_expert(e_i)
            mr = stage_mrow(e_i, maskTb, T)
            for ts in range(4):
                t0 = 512 * ts
                he_ve = build_he(e_i, t0, 512, ew, maskTb, mr)
                for t128 in range(4):
                    tch = 4 * ts + t128
                    for dh in range(2):
                        bank = pf[2 + (t128 * 2 + dh) % 2]
                        for hc in range(8):
                            w = None
                            if hc == 0:
                                w = [("VE", he_ve if (t128 == 0 and dh == 0)
                                      else p.now("VE"))]
                            mm(bank[:, :], he[hc][:, 128 * t128:128 * (t128 + 1)],
                               w2b[hc][:, 512 * dh:512 * (dh + 1)],
                               start=(hc == 0), stop=(hc == 7),
                               inc="TE" if hc == 7 else None, wait=w)
                        te = p.now("TE")
                        vop(lambda e, tch=tch, dh=dh, bank=bank: e.tensor_tensor(
                            xstream[tch][:, 512 * dh:512 * (dh + 1)], bank[:, :],
                            xstream[tch][:, 512 * dh:512 * (dh + 1)], ALU.add),
                            wait=[("TE", te)])
        for tch in range(16):
            for dh in range(2):
                bank = pf[2 + (tch * 2 + dh) % 2]
                mm(bank[:, :], maskTb[0:8, 128 * tch:128 * (tch + 1)],
                   b2sb[0:8, 512 * dh:512 * (dh + 1)], start=True, stop=True,
                   inc="TE", wait=[("VE", p.now("VE"))] if dh == 0 else None)
                te = p.now("TE")
                vop(lambda e, tch=tch, dh=dh, bank=bank: e.tensor_tensor(
                    xstream[tch][:, 512 * dh:512 * (dh + 1)], bank[:, :],
                    xstream[tch][:, 512 * dh:512 * (dh + 1)], ALU.add),
                    wait=[("TE", te)])

        if _K.MOE_CUT < 5:
            E("gpsimd", wait=[("CC", ar_cc)])
            abf.release(mkM)
            return
        E("gpsimd", wait=[("CC", ar_cc)])
        dma("gpsimd", "GL", s01g[0:8, 0:1], dram["ar_out"][0:8, 0:1])
        dma("gpsimd", "GL", s01g2[0:8, 0:1], dram["ar_out"][8:16, 0:1])
        glc = p.now("GL")
        vop(lambda e: e.tensor_copy(mcf[0:8, :], mrows[0:8, :]),
            wait=[("GL", glc)])
        vsync()
        vop(lambda e: e.tensor_tensor(mcf[0:8, 0:1], mcf[0:8, 0:1],
                                      s01g[0:8, 0:1], ALU.add))
        vop(lambda e: e.tensor_tensor(mcf[0:8, 1:2], mcf[0:8, 1:2],
                                      s01g2[0:8, 0:1], ALU.add))
        vsync()
        vop(lambda e: e.tensor_scalar(mcf[0:8, :], mcf[0:8, :], CAP, None,
                                      ALU.min))
        vsync()
        vop(lambda e: e.tensor_copy(mcb8[0:8, :], mcf[0:8, :]))
        ve = p.now("VE")
        tr(pbf[0:8, 0:8], mcb8[0:8, 0:8], wait=[("VE", ve)],
           idn=s_ident[0:8, 0:8])
        te = p.now("TE")
        vop(lambda e: e.tensor_copy(mcbT[0:8, 0:8], pbf[0:8, 0:8]),
            wait=[("TE", te)])
        nacc = [0, 0, 0, 0]
        for e_i in range(2):
            ew = load_expert(e_i, extra_wait=[("VE", p.now("VE"))])
            for variant, mfm in ((0, mcbT), (1, maskTb)):
                mr = stage_mrow(e_i, mfm, 8)
                he_ve = build_he(e_i, 0, 8, ew, mfm, mr)
                for dh in range(2):
                    bank = pf[4 + dh] if variant == 0 else pf[2 + dh]
                    k = variant * 2 + dh
                    for hc in range(8):
                        mm(bank[0:8, :], he[hc][:, 0:8],
                           w2b[hc][:, 512 * dh:512 * (dh + 1)],
                           start=(nacc[k] == 0), stop=False,
                           inc="TE" if hc == 7 else None,
                           wait=[("VE", he_ve)] if hc == 0 else None)
                        nacc[k] += 1
        for variant, mfm in ((0, mcbT), (1, maskTb)):
            for dh in range(2):
                bank = pf[4 + dh] if variant == 0 else pf[2 + dh]
                mm(bank[0:8, :], mfm[0:2, 0:8], b2sb[0:2, 512 * dh:512 * (dh + 1)],
                   start=False, stop=True, inc="TE")
        te = p.now("TE")
        for dh in range(2):
            vop(lambda e, dh=dh: e.tensor_copy(dscr[0:8, :], pf[4 + dh][0:8, :]),
                wait=[("TE", te)] if dh == 0 else None)
            vsync()
            vop(lambda e, dh=dh: e.tensor_tensor(dscr[0:8, :], dscr[0:8, :],
                                                 pf[2 + dh][0:8, :], ALU.subtract))
            vsync()
            vop(lambda e, dh=dh: e.tensor_scalar(dscr[0:8, :], dscr[0:8, :],
                                                 s_isc0[0:8, 0:1], None, ALU.mult))
            vsync()
            vop(lambda e, dh=dh: e.tensor_tensor(
                xstream[0][0:8, 512 * dh:512 * (dh + 1)], dscr[0:8, :],
                xstream[0][0:8, 512 * dh:512 * (dh + 1)], ALU.add))
        abf.release(mkM)

    if stage >= 2:
        moe_phase()
    x2_ve = p.now("VE")

    if debug:
        for tch in range(16):
            dma("gpsimd", "SG", dram["dbg_x2"][128 * tch:128 * (tch + 1), :],
                xstream[tch][:, :], wait=[("VE", x2_ve)] if tch == 0 else None)
        E("vector", wait=[("SG", p.now("SG"))])

    for tch in range(16):
        dma("gpsimd", "GL", dram["ag_in"][128 * tch:128 * (tch + 1), :],
            xstream[tch][:, :], wait=[("VE", x2_ve)] if tch == 0 else None)
    gl = p.now("GL")
    E("gpsimd", lambda e: e.collective_compute(
        "AllGather", ALU.bypass,
        replica_groups=[[0, 1], [2, 3], [4, 5], [6, 7]],
        ins=[dram["ag_in"]], outs=[dram["ag_out"]]),
      inc="CC", wait=[("GL", gl)])
    ag_cc = p.now("CC")

    def l1_attn_phase():
        mkL1 = abf.mark()
        alloc_sweepbufs()
        Qb = abf.alloc_fm(8, T)
        mkI = abf.mark()
        wtb = abf.alloc_fm(8, D)
        E("gpsimd", wait=[("WL", p.now("WL")), ("VE", p.now("VE")),
                          ("TE", p.now("TE")), ("SE", p.now("SE"))])
        for i in range(8):
            dma("gpsimd", "WL", wtb[i][:, :], l1wqT[128 * i:128 * (i + 1), :])
        qw = p.now("WL")

        def q_src(tc):
            return xstream[tc], None

        def q_group(g):
            grp = SW["xnT"]
            for fch in range(8):
                bank = pf[fch % 2]
                for dch in range(8):
                    w = None
                    if dch == 0:
                        w = [("WL", qw), ("VE", p.now("VE"))]
                    mm(bank[:, :], wtb[dch][:, 128 * fch:128 * (fch + 1)],
                       grp[dch][:, 0:512], start=(dch == 0), stop=(dch == 7),
                       inc="TE" if dch == 7 else None, wait=w)
                te = p.now("TE")
                vop(lambda e, fch=fch, g=g, bank=bank: e.tensor_copy(
                    Qb[fch][:, 512 * g:512 * (g + 1)], bank[:, :]),
                    wait=[("TE", te)])

        sweep(16, q_src, q_group)
        abf.release(mkI)

        mkI2 = abf.mark()
        wkb_ = abf.alloc_fm(8, D)
        wvb_ = abf.alloc_fm(8, D)
        Kq = abf.alloc_fm(8, 1024)
        Vq = abf.alloc_fm(8, D)
        PTq = abf.alloc_fm(8, 512)
        sq2 = abf.alloc(512)
        E("gpsimd", wait=[("WL", p.now("WL")), ("VE", p.now("VE")),
                          ("TE", p.now("TE")), ("SE", p.now("SE"))])
        for i in range(8):
            dma("gpsimd", "WL", wkb_[i][:, :], l1wkT[128 * i:128 * (i + 1), :])
            dma("gpsimd", "WL", wvb_[i][:, :], l1wvT[128 * i:128 * (i + 1), :])
        kvw = p.now("WL")

        for i in range(32):
            fch, half = i // 4, i % 4
            sl_ = Qb[fch][:, 512 * half:512 * (half + 1)]
            vop(lambda e, sl_=sl_: e.tensor_tensor(sq2[:, :], sl_, sl_, ALU.mult),
                wait=[("TE", p.now("TE"))] if i == 0 else None)
            ve = p.now("VE")
            mm(pf[4][0:1, :], s_ones[0:128, 0:1], sq2[:, :], start=True, stop=True,
               inc="TE", wait=[("VE", ve)])
            te = p.now("TE")
            vop(lambda e: e.reduce_max(csc[0:1, 1:2], pf[4][0:1, :], AX),
                wait=[("TE", te)])
            vsync()
            if i == 0:
                vop(lambda e: e.tensor_copy(csc[0:1, 0:1], csc[0:1, 1:2]))
            else:
                vop(lambda e: e.tensor_tensor(csc[0:1, 0:1], csc[0:1, 0:1],
                                              csc[0:1, 1:2], ALU.max))
            vsync()
            E("tensor", wait=[("VE", p.now("VE"))])

        gath = dram["ag_out"]
        for kq in range(4):
            def kv_src(tc, kq=kq):
                st = xstage[tc % 2]
                psem = "XA" if tc % 2 == 0 else "XB"
                dma("sync", psem, st[:, :],
                    gath[1024 * kq + 128 * tc:1024 * kq + 128 * (tc + 1), :],
                    wait=[("VE", p.now("VE"))])
                return st, [(psem, p.now(psem))]

            def kv_group(g, kq=kq):
                grp = SW["xnT"]
                for fch in range(8):
                    bank = pf[fch % 2]
                    for dch in range(8):
                        w = None
                        if dch == 0:
                            w = [("WL", kvw), ("VE", p.now("VE"))]
                        mm(bank[:, :], wkb_[dch][:, 128 * fch:128 * (fch + 1)],
                           grp[dch][:, 0:512], start=(dch == 0), stop=(dch == 7),
                           inc="TE" if dch == 7 else None, wait=w)
                    te = p.now("TE")
                    vop(lambda e, fch=fch, g=g, bank=bank: e.tensor_copy(
                        Kq[fch][:, 512 * g:512 * (g + 1)], bank[:, :]),
                        wait=[("TE", te)])
                for kc in range(4):
                    for eh in range(2):
                        bank = pf[2 + eh]
                        for dch in range(8):
                            w = [("VE", p.now("VE"))] if dch == 0 else None
                            mm(bank[:, :], grp[dch][:, 128 * kc:128 * (kc + 1)],
                               wvb_[dch][:, 512 * eh:512 * (eh + 1)],
                               start=(dch == 0), stop=(dch == 7),
                               inc="TE" if dch == 7 else None, wait=w)
                        te = p.now("TE")
                        vop(lambda e, g=g, kc=kc, eh=eh, bank=bank: e.tensor_copy(
                            Vq[g * 4 + kc][:, 512 * eh:512 * (eh + 1)], bank[:, :]),
                            wait=[("TE", te)])

            sweep(8, kv_src, kv_group)

            for i in range(16):
                fch, half = i // 2, i % 2
                sl_ = Kq[fch][:, 512 * half:512 * (half + 1)]
                vop(lambda e, sl_=sl_: e.tensor_tensor(sq2[:, :], sl_, sl_, ALU.mult))
                ve = p.now("VE")
                mm(pf[4][0:1, :], s_ones[0:128, 0:1], sq2[:, :], start=True,
                   stop=True, inc="TE", wait=[("VE", ve)])
                te = p.now("TE")
                vop(lambda e: e.reduce_max(csc[0:1, 3:4], pf[4][0:1, :], AX),
                    wait=[("TE", te)])
                vsync()
                if i == 0:
                    vop(lambda e: e.tensor_copy(csc[0:1, 2:3], csc[0:1, 3:4]))
                else:
                    vop(lambda e: e.tensor_tensor(csc[0:1, 2:3], csc[0:1, 2:3],
                                                  csc[0:1, 3:4], ALU.max))
                vsync()
                E("tensor", wait=[("VE", p.now("VE"))])
            vop(lambda e: e.tensor_tensor(csc[0:1, 4:5], csc[0:1, 0:1],
                                          csc[0:1, 2:3], ALU.add))
            vsync()
            vop(lambda e: e.tensor_scalar(csc[0:1, 5:6], csc[0:1, 4:5],
                                          -1.0 / 16.0, None, ALU.mult))
            vsync()
            if kq == 0:
                vop(lambda e: e.tensor_copy(csc[0:1, 6:7], csc[0:1, 5:6]))
                vsync()
            else:
                vop(lambda e: e.tensor_tensor(csc[0:1, 7:8], csc[0:1, 6:7],
                                              csc[0:1, 5:6], ALU.min))
                vsync()
                vop(lambda e: e.tensor_scalar(csc[0:1, 8:9], csc[0:1, 6:7],
                                              -1.0, None, ALU.mult))
                vsync()
                vop(lambda e: e.tensor_tensor(csc[0:1, 8:9], csc[0:1, 7:8],
                                              csc[0:1, 8:9], ALU.add))
                ve = p.now("VE")
                sop(lambda e: e.activation(csc[0:1, 9:10], csc[0:1, 8:9], AF.Exp),
                    wait=[("VE", ve)])
                se = p.now("SE")
                vop(lambda e: e.tensor_copy(csc[0:1, 6:7], csc[0:1, 7:8]),
                    wait=[("SE", se)])
                mm(pf[4][:, 0:1], s_onesrow[0:1, :], csc[0:1, 9:10], start=True,
                   stop=True, inc="TE", wait=[("VE", p.now("VE"))])
                te = p.now("TE")
                vop(lambda e: e.tensor_copy(csc[:, 10:11], pf[4][:, 0:1]),
                    wait=[("TE", te)])
                vsync()
                for tch in range(16):
                    vop(lambda e, tch=tch: e.tensor_scalar(
                        xstream[tch][:, :], xstream[tch][:, :], csc[:, 10:11],
                        None, ALU.mult))
                vop(lambda e: e.tensor_scalar(sumsacc[:, 0:16], sumsacc[:, 0:16],
                                              csc[:, 10:11], None, ALU.mult))
            ve = p.now("VE")
            mm(pf[4][:, 0:1], s_onesrow[0:1, :], csc[0:1, 6:7], start=True,
               stop=True, inc="TE", wait=[("VE", ve)])
            te = p.now("TE")
            vop(lambda e: e.tensor_copy(cb[:, 0:1], pf[4][:, 0:1]),
                wait=[("TE", te), ("SE", p.now("SE"))])
            cb_ve = p.now("VE")
            E("tensor", wait=[("VE", cb_ve)])

            for qc in range(4):
                q0 = 512 * qc
                se_base = p.now("SE")
                exp_marks = [se_base, se_base]
                for kt in range(8):
                    par = kt % 2
                    sb = pf[par]
                    wt = [("SE", exp_marks[par])]
                    if kt == 0:
                        wt.append(("VE", p.now("VE")))
                    for ech in range(8):
                        mm(sb[:, :], Kq[ech][:, 128 * kt:128 * (kt + 1)],
                           Qb[ech][:, q0:q0 + 512], start=(ech == 0),
                           stop=(ech == 7), inc="TE" if ech == 7 else None,
                           wait=wt if ech == 0 else None)
                    te = p.now("TE")
                    sop(lambda e, par=par, sb=sb, kt=kt: e.activation(
                        PTq[kt][:, :], sb[:, :], AF.Exp, bias=cb[:, 0:1],
                        scale=0.125), wait=[("TE", te)])
                    exp_marks[par] = p.now("SE")
                se = p.now("SE")
                for q128 in range(4):
                    tch = 4 * qc + q128
                    for eh in range(2):
                        bank = pf[2 + eh]
                        for kt in range(8):
                            w = None
                            if kt == 0:
                                w = [("SE", se), ("VE", p.now("VE"))]
                            mm(bank[:, :], PTq[kt][:, 128 * q128:128 * (q128 + 1)],
                               Vq[kt][:, 512 * eh:512 * (eh + 1)],
                               start=(kt == 0), stop=(kt == 7),
                               inc="TE" if kt == 7 else None, wait=w)
                    for kt in range(8):
                        mm(pf[5][:, 0:1], PTq[kt][:, 128 * q128:128 * (q128 + 1)],
                           s_ones[0:128, 0:1], start=(kt == 0), stop=(kt == 7),
                           inc="TE" if kt == 7 else None,
                           wait=[("VE", p.now("VE"))] if kt == 0 else None)
                    te = p.now("TE")
                    if kq == 0:
                        for eh in range(2):
                            w0 = [("TE", te)] if eh == 0 else None
                            if eh == 0 and qc == 0 and q128 == 0:
                                w0 = [("TE", te), ("GL", p.now("GL"))]
                            vop(lambda e, tch=tch, eh=eh: e.tensor_copy(
                                xstream[tch][:, 512 * eh:512 * (eh + 1)],
                                pf[2 + eh][:, :]), wait=w0)
                        vop(lambda e, tch=tch: e.tensor_copy(
                            sumsacc[:, tch:tch + 1], pf[5][:, 0:1]))
                    else:
                        for eh in range(2):
                            vop(lambda e, tch=tch, eh=eh: e.tensor_tensor(
                                xstream[tch][:, 512 * eh:512 * (eh + 1)],
                                pf[2 + eh][:, :],
                                xstream[tch][:, 512 * eh:512 * (eh + 1)], ALU.add),
                                wait=[("TE", te)] if eh == 0 else None)
                        vop(lambda e, tch=tch: e.tensor_tensor(
                            sumsacc[:, tch:tch + 1], pf[5][:, 0:1],
                            sumsacc[:, tch:tch + 1], ALU.add))
                    E("tensor", wait=[("VE", p.now("VE"))])

        abf.release(mkI2)
        O_bf = abf.alloc_fm(16, D)
        for tch in range(16):
            vop(lambda e, tch=tch: e.reciprocal(csc[:, 11:12],
                                                sumsacc[:, tch:tch + 1]))
            vsync()
            vop(lambda e, tch=tch: e.tensor_scalar(
                O_bf[tch][:, :], xstream[tch][:, :], csc[:, 11:12], None,
                ALU.mult))
        obf_ve = p.now("VE")

        wob = abf.alloc_fm(8, D)
        E("gpsimd", wait=[("WL", p.now("WL")), ("VE", obf_ve),
                          ("TE", p.now("TE")), ("SE", p.now("SE"))])
        for i in range(8):
            dma("gpsimd", "WL", wob[i][:, :], l1woT[128 * i:128 * (i + 1), :])
        wow = p.now("WL")
        rbh = {}
        E("gpsimd", wait=[("CN", consts_xl)])
        p.q["gpsimd"].append(lambda e: rbh.__setitem__("v", e.value_load(
            s_cpar[0:1, 0:1])))
        gath4 = dram["ag_out"].rearrange("(a r) d -> a r d", r=4)
        for rl in range(2):
            for ac in range(8):
                tch = rl * 8 + ac
                psem = "GA" if tch % 2 == 0 else "GB"
                E("gpsimd", wait=[("VE", p.now("VE"))])
                pv = E("gpsimd", lambda e, rl=rl, ac=ac, tch=tch: e.dma_start(
                    out=xstage[tch % 2][:, :],
                    in_=gath4[bass.ds(ac * 128, 128),
                              bass.ds(rbh["v"] + rl, 1), :]),
                    inc=psem, incby=16)
                for oh in range(2):
                    bank = pf[(tch * 2 + oh) % 2]
                    for dch in range(8):
                        w = None
                        if dch == 0:
                            w = [("WL", wow), ("VE", p.now("VE"))]
                        mm(bank[:, :], O_bf[rl * 8 + dch][:, 128 * ac:128 * (ac + 1)],
                           wob[dch][:, 512 * oh:512 * (oh + 1)],
                           start=(dch == 0), stop=(dch == 7),
                           inc="TE" if dch == 7 else None, wait=w)
                    te = p.now("TE")
                    vop(lambda e, tch=tch, oh=oh, bank=bank: e.tensor_tensor(
                        xstream[tch][:, 512 * oh:512 * (oh + 1)], bank[:, :],
                        xstage[tch % 2][:, 512 * oh:512 * (oh + 1)], ALU.add),
                        wait=[("TE", te), (psem, pv)])
        abf.release(mkL1)

    if stage >= 3:
        l1_attn_phase()
    else:
        gath4 = dram["ag_out"].rearrange("(a r) d -> a r d", r=4)
        semh = p.semh
        def reown(e):
            rb = e.value_load(s_cpar[0:1, 0:1])
            for rl in range(2):
                for ac in range(8):
                    tch = rl * 8 + ac
                    e.dma_start(out=xstream[tch][:, :],
                                in_=gath4[bass.ds(ac * 128, 128),
                                          bass.ds(rb + rl, 1), :]
                                ).then_inc(semh["GL"], 16)
        E("gpsimd", wait=[("CC", ag_cc), ("CN", consts_xl), ("VE", x2_ve)])
        p.cnt["GL"] += 16 * 16
        p.q["gpsimd"].append(reown)
    x3_gl = p.now("GL")
    x3_ve = p.now("VE")

    if debug:
        for tch in range(16):
            dma("gpsimd", "SG", dram["dbg_x3"][128 * tch:128 * (tch + 1), :],
                xstream[tch][:, :],
                wait=[("GL", x3_gl), ("VE", x3_ve)] if tch == 0 else None)
        E("vector", wait=[("SG", p.now("SG"))])

    def l1_ffn_phase():
        mkF = abf.mark()
        alloc_sweepbufs()
        xn3T = abf.alloc_fm(8, T)

        def f_src(tc):
            return xstream[tc], None

        def f_dst(t0):
            return [View(v.t, v.off + 128 * t0, 128, 128) for v in xn3T]

        sweep(16, f_src, None, dst_of=f_dst)

        hh = abf.alloc_fm(8, T)
        wgu = [abf.alloc_fm(8, 512), abf.alloc_fm(8, 512)]
        wdball = abf.alloc_fm(8, D)
        gt2 = abf.alloc(512)
        islot = [0]
        islot_ve = [0, 0, 0]

        def i32_load(dst_bf, src_i32_ap, nel):
            sl = islot[0] % 3
            islot[0] += 1
            psem = ("IA", "IB")[sl % 2]
            dma("sync", psem, s_i32[:, 512 * sl:512 * sl + nel], src_i32_ap,
                wait=[("VE", islot_ve[sl])])
            lv = p.now(psem)
            vop(lambda e, sl=sl, nel=nel, d=dst_bf: e.tensor_scalar(
                d, s_i32[:, 512 * sl:512 * sl + nel], 1.0 / 127.0, None,
                ALU.mult), wait=[(psem, lv), ("TE", p.now("TE"))])
            islot_ve[sl] = p.now("VE")

        for hf in range(3):
            h0 = 1024 * hf
            for hb in range(2):
                for j in range(2):
                    wsrc = wgT if j == 0 else wuT
                    for dch in range(8):
                        i32_load(wgu[j][dch][:, 0:512],
                                 wsrc[128 * dch:128 * (dch + 1),
                                      h0 + 512 * hb:h0 + 512 * (hb + 1)], 512)
                for hc4 in range(4):
                    hloc = hb * 4 + hc4
                    for tq in range(4):
                        bank = pf[tq % 2]
                        for dch in range(8):
                            w = [("VE", p.now("VE")), ("SE", p.now("SE"))] \
                                if dch == 0 else None
                            mm(bank[:, :], wgu[0][dch][:, 128 * hc4:128 * (hc4 + 1)],
                               xn3T[dch][:, 512 * tq:512 * (tq + 1)],
                               start=(dch == 0), stop=(dch == 7),
                               inc="TE" if dch == 7 else None, wait=w)
                        te = p.now("TE")
                        sop(lambda e, bank=bank: e.activation(
                            gt2[:, :], bank[:, :], AF.Silu), wait=[("TE", te)])
                        se = p.now("SE")
                        bank2 = pf[2 + tq % 2]
                        for dch in range(8):
                            w = [("SE", se), ("VE", p.now("VE"))] \
                                if dch == 0 else None
                            mm(bank2[:, :], wgu[1][dch][:, 128 * hc4:128 * (hc4 + 1)],
                               xn3T[dch][:, 512 * tq:512 * (tq + 1)],
                               start=(dch == 0), stop=(dch == 7),
                               inc="TE" if dch == 7 else None, wait=w)
                        te = p.now("TE")
                        vop(lambda e, hloc=hloc, tq=tq, bank2=bank2: e.tensor_tensor(
                            hh[hloc][:, 512 * tq:512 * (tq + 1)], gt2[:, :],
                            bank2[:, :], ALU.mult), wait=[("TE", te)])
            for hloc in range(8):
                for dh in range(2):
                    i32_load(wdball[hloc][:, 512 * dh:512 * (dh + 1)],
                             wdT[h0 + 128 * hloc:h0 + 128 * (hloc + 1),
                                 512 * dh:512 * (dh + 1)], 512)
            wd_ve = p.now("VE")
            last = (hf == 2)
            for t128 in range(16):
                for dh in range(2):
                    bank = pf[4 + (t128 * 2 + dh) % 2]
                    for hloc in range(8):
                        w = None
                        if hloc == 0:
                            w = [("VE", wd_ve if (t128 == 0 and dh == 0)
                                  else p.now("VE"))]
                        mm(bank[:, :], hh[hloc][:, 128 * t128:128 * (t128 + 1)],
                           wdball[hloc][:, 512 * dh:512 * (dh + 1)],
                           start=(hloc == 0), stop=(hloc == 7),
                           inc="TE" if hloc == 7 else None, wait=w)
                    te = p.now("TE")
                    vop(lambda e, t128=t128, dh=dh, bank=bank: e.tensor_tensor(
                        xstream[t128][:, 512 * dh:512 * (dh + 1)], bank[:, :],
                        xstream[t128][:, 512 * dh:512 * (dh + 1)], ALU.add),
                        wait=[("TE", te)])
                if last:
                    yv = p.now("VE")
                    dma("sync", "ST", dram["y"][128 * t128:128 * (t128 + 1), :],
                        xstream[t128][:, :], wait=[("VE", yv)])
        abf.release(mkF)

    if stage >= 4:
        l1_ffn_phase()
    else:
        for tch in range(16):
            dma("sync", "ST", dram["y"][128 * tch:128 * (tch + 1), :],
                xstream[tch][:, :],
                wait=[("GL", x3_gl), ("VE", x3_ve)] if tch == 0 else None)

    E("gpsimd", wait=[("ST", p.now("ST")), ("SG", p.now("SG"))])
    p.q["gpsimd"].append(lambda e: e.nop())

    p.replay()
    ctx.close()
    return nc


def _rope_tables():
    inv = 1.0 / 10000.0 ** (np.arange(0, HD, 2, dtype=np.float64) / HD)
    C = np.ones(512, np.float64)
    Sn = np.zeros(512, np.float64)
    for h in range(NH):
        for j in range(DROPE // 2):
            C[h * 32 + j] = np.cos(h * inv[j])
            Sn[h * 32 + j] = np.sin(h * inv[j])
    C = np.ascontiguousarray(np.tile(C.astype(BF)[None, :], (128, 1)))
    Sn = np.ascontiguousarray(np.tile(Sn.astype(BF)[None, :], (128, 1)))
    return C, Sn


_NC_CACHE = {}


def kernel(**inputs):
    key = (STAGE, DEBUG)
    if key not in _NC_CACHE:
        _NC_CACHE[key] = build_program(STAGE, DEBUG)
    nc = _NC_CACHE[key]

    f32 = np.float32
    tT = lambda a: np.ascontiguousarray(np.asarray(a).T)
    cc = np.ascontiguousarray
    ropeC, ropeS = _rope_tables()
    shared = {
        "wqT": tT(inputs["L0_attn_wq"]),
        "wkaT": tT(inputs["L0_attn_wk_a"]),
        "wkbT": tT(inputs["L0_attn_wk_b"]),
        "wvaT": tT(inputs["L0_attn_wv_a"]),
        "wvbT": tT(inputs["L0_attn_wv_b"]),
        "woT": tT(inputs["L0_attn_wo"]),
        "router_w": cc(np.asarray(inputs["L0_ffn_router_w"])),
        "router_b": cc(np.asarray(inputs["L0_ffn_router_b"]).reshape(1, NE)),
        "w1": cc(np.asarray(inputs["L0_ffn_w1"])),
        "b1": cc(np.asarray(inputs["L0_ffn_b1"])),
        "w2": cc(np.asarray(inputs["L0_ffn_w2"])),
        "b2": cc(np.asarray(inputs["L0_ffn_b2"])),
        "l1wqT": tT(inputs["L1_attn_wq"]),
        "l1wkT": tT(inputs["L1_attn_wk"]),
        "l1wvT": tT(inputs["L1_attn_wv"]),
        "l1woT": tT(inputs["L1_attn_wo"]),
        "wgT": tT(inputs["L1_ffn_w_gate"]),
        "wuT": tT(inputs["L1_ffn_w_up"]),
        "wdT": tT(inputs["L1_ffn_w_down"]),
        "identc": np.eye(128, dtype=BF),
        "onesc": np.ones((128, 1), BF),
        "onesrow": np.ones((1, 128), f32),
        "onesrowb": np.ones((1, 128), BF),
        "b1T": tT(inputs["L0_ffn_b1"]),
        "ropeC": ropeC,
        "ropeS": ropeS,
    }
    x = np.asarray(inputs["L0_attn_x"])
    in_maps = []
    for c in range(8):
        b, h = c // 2, c % 2
        m = dict(shared)
        m["x_own"] = cc(x[b, h * T:(h + 1) * T, :])
        m["x_sib"] = cc(x[b, (1 - h) * T:(2 - h) * T, :])
        m["iscore0"] = np.full((128, 1), 1.0 if c == 0 else 0.0, f32)
        m["cparams"] = np.array([[2 * h, 0]], np.int32)
        in_maps.append(m)

    res = run_bass_kernel_spmd(nc, in_maps, core_ids=list(range(8)))

    out = np.empty((B, S, D), f32)
    for c in range(8):
        b, h = c // 2, c % 2
        yv = np.asarray(res.results[c]["y"]).reshape(2, 1024, D)
        for rl in range(2):
            out[b, (2 * h + rl)::4, :] = yv[rl]
    if DEBUG:
        kernel._dbg = res
    return out
